# version 07
# baseline (speedup 1.0000x reference)
import numpy as np
import ml_dtypes
import concourse.bass as bass
import concourse.mybir as mybir
from concourse.bass_utils import run_bass_kernel_spmd
from contextlib import ExitStack

F32 = mybir.dt.float32
BF16 = mybir.dt.bfloat16
I32 = mybir.dt.int32
AF = mybir.ActivationFunctionType
ALU = mybir.AluOpType
AX = mybir.AxisListType.X

P = 128
D = 2048
ND = 16
II = 8192
NI = 64
TCORE = 4096
TC = 512
NCH = 8
NT = 4
EPS = 1e-5


class _StubInstr:
    def then_inc(self, *a, **k):
        return self


class _StubEngine:
    def __getattr__(self, name):
        def f(*a, **k):
            return _StubInstr()
        return f


class _EvDict(dict):
    pass2 = False

    def __missing__(self, key):
        if self.pass2:
            raise KeyError(key)
        return 0


def build_nc():
    nc = bass.Bass("TRN2", target_bir_lowering=False)

    x_in = nc.dram_tensor("x", [TCORE, D], F32, kind="ExternalInput").ap()
    av = nc.dram_tensor("av", [1, D], F32, kind="ExternalInput").ap()
    bv = nc.dram_tensor("bv", [1, D], F32, kind="ExternalInput").ap()
    cv = nc.dram_tensor("cv", [1, D], F32, kind="ExternalInput").ap()
    dv = nc.dram_tensor("dv", [1, D], F32, kind="ExternalInput").ap()
    gbv = nc.dram_tensor("gbv", [1, D], F32, kind="ExternalInput").ap()
    dscv = nc.dram_tensor("dscv", [1, D], F32, kind="ExternalInput").ap()
    gscv = nc.dram_tensor("gscv", [1, II], F32, kind="ExternalInput").ap()
    uscv = nc.dram_tensor("uscv", [1, II], F32, kind="ExternalInput").ap()
    gate_w = nc.dram_tensor("gate_w", [D, D], F32, kind="ExternalInput").ap()
    gq = nc.dram_tensor("gq", [II, D], I32, kind="ExternalInput").ap()
    uq = nc.dram_tensor("uq", [II, D], I32, kind="ExternalInput").ap()
    dq = nc.dram_tensor("dq", [D, II], I32, kind="ExternalInput").ap()
    dlru = nc.dram_tensor("dlru", [D, 4], F32, kind="ExternalInput").ap()
    dlrv = nc.dram_tensor("dlrv", [4, D], F32, kind="ExternalInput").ap()
    identf = nc.dram_tensor("identf", [P, P], F32, kind="ExternalInput").ap()
    identb = nc.dram_tensor("identb", [P, P], BF16, kind="ExternalInput").ap()
    masks = nc.dram_tensor("masks", [1, 64], F32, kind="ExternalInput").ap()

    out_d = nc.dram_tensor("out", [TCORE, D], F32, kind="ExternalOutput").ap()

    wgs = nc.dram_tensor("wgs", [ND, P, D], BF16).ap()
    wgfs = nc.dram_tensor("wgfs", [NI, P, D], BF16).ap()
    wufs = nc.dram_tensor("wufs", [NI, P, D], BF16).ap()
    wdfs = nc.dram_tensor("wdfs", [ND, P, II], BF16).ap()
    xnts = nc.dram_tensor("xnts", [NCH, P, ND * TC], BF16).ap()
    ugs = nc.dram_tensor("ugs", [NCH, P, ND * TC], BF16).ap()
    ccin_ap = nc.dram_tensor("ccin", [2, D], F32).ap()
    ccout_ap = nc.dram_tensor("ccout", [16, D], F32).ap()

    with ExitStack() as ctx:
        e = ctx.enter_context

        h = e(nc.sbuf_tensor("h", [P, NI * TC], BF16))
        hF = h.bitcast(F32)
        hI = h.bitcast(I32)
        ug = e(nc.sbuf_tensor("ug", [P, ND * TC], BF16))
        ugF = ug.bitcast(F32)
        xnT = e(nc.sbuf_tensor("xnT", [P, ND * TC], BF16))
        xnTF = xnT.bitcast(F32)

        def delta_ap(j2, lo, width):
            reg = xnTF if j2 % 2 == 0 else ugF
            base = (j2 // 2) * TC
            return reg[:, base + lo: base + lo + width]
        x_rot = [e(nc.sbuf_tensor(f"xr{i}", [P, D], F32)) for i in range(2)]
        tok_rot = [e(nc.sbuf_tensor(f"tr{i}", [P, D], F32)) for i in range(2)]
        wstr_g = [e(nc.sbuf_tensor(f"wg{i}", [P, D], BF16)) for i in range(2)]
        wstr_u = [e(nc.sbuf_tensor(f"wu{i}", [P, D], BF16)) for i in range(2)]
        wstr_d = [e(nc.sbuf_tensor(f"wd{i}", [P, 4096], BF16)) for i in range(2)]
        sg_rot = [e(nc.sbuf_tensor(f"sg{i}", [P, TC], F32)) for i in range(2)]
        ut_rot = [e(nc.sbuf_tensor(f"ut{i}", [P, TC], F32)) for i in range(2)]
        sc_rot = [e(nc.sbuf_tensor(f"sc{i}", [P, TC], F32)) for i in range(2)]
        gf_rot = [e(nc.sbuf_tensor(f"gf{i}", [P, TC], F32)) for i in range(2)]
        y_rot = [e(nc.sbuf_tensor(f"yy{i}", [P, TC], F32)) for i in range(2)]

        idf = e(nc.sbuf_tensor("idf", [P, P], F32))
        idb = e(nc.sbuf_tensor("idb", [P, P], BF16))
        cols = e(nc.sbuf_tensor("cols", [P, 16 * 12], F32))
        gscc = e(nc.sbuf_tensor("gscc", [P, NI], F32))
        uscc = e(nc.sbuf_tensor("uscc", [P, NI], F32))
        dlruc = e(nc.sbuf_tensor("dlruc", [P, 64], F32))
        dlrvc = e(nc.sbuf_tensor("dlrvc", [4, D], F32))
        maskc = e(nc.sbuf_tensor("maskc", [1, 64], F32))
        onesr = e(nc.sbuf_tensor("onesr", [1, P], F32))
        mbc = e(nc.sbuf_tensor("mbc", [P, 64], F32))
        qm = e(nc.sbuf_tensor("qm", [P, 64], F32))
        jk16 = e(nc.sbuf_tensor("jk16", [P, 16], F32))
        pwa = e(nc.sbuf_tensor("pwa", [P, 16], F32))
        pwb = e(nc.sbuf_tensor("pwb", [P, 16], F32))
        a512 = e(nc.sbuf_tensor("a512", [P, 16], F32))
        pw1 = e(nc.sbuf_tensor("pw1", [P, 16], F32))
        pw2 = e(nc.sbuf_tensor("pw2", [P, 16], F32))
        sin_t = e(nc.sbuf_tensor("sin_t", [P, 16], F32))
        cura = e(nc.sbuf_tensor("cura", [P, 16], F32))
        curb = e(nc.sbuf_tensor("curb", [P, 16], F32))
        rsb = e(nc.sbuf_tensor("rsb", [4, 1], F32))
        carr = e(nc.sbuf_tensor("carr", [P, 16], F32))
        carrT = e(nc.sbuf_tensor("carrT", [16, P], F32))
        ustotT = e(nc.sbuf_tensor("ustotT", [16, P], F32))
        carrh = e(nc.sbuf_tensor("carrh", [P, 16 * 9], F32))
        sf = e(nc.sbuf_tensor("sf", [P, 16 * 8], F32))
        convb = e(nc.sbuf_tensor("convb", [P, 2 * D], BF16))
        vrow = convb.bitcast(F32)[0:1, 0:D]
        usall = e(nc.sbuf_tensor("usall", [P, 128], F32))
        lnst = e(nc.sbuf_tensor("lnst", [P, 16], F32))
        epst = e(nc.sbuf_tensor("epst", [P, 1], F32))
        gcol = hF[0:16, 1024:1024 + D]

        psF = [e(nc.psum_tensor(f"psF{i}", [P, TC], F32)) for i in range(6)]
        psT = [e(nc.psum_tensor(f"psT{i}", [P, 2 * TC], BF16)) for i in range(2)]

        sems = {k: e(nc.semaphore(f"s_{k}")) for k in
                ("pe", "dve", "act", "cc",
                 "cst", "vrw", "xl0", "xl1",
                 "wi0", "wi1", "wi2", "wi3", "wi4", "wi5", "wi6", "wi7",
                 "wg0", "wg1", "wu0", "wu1", "wd0", "wd1",
                 "xnl", "ugl", "cci", "ccg",
                 "xst", "ust", "wo0", "wo1", "os0", "os1",
                 "cx0", "cx1", "plc")}
        block = e(nc.Block())
        n = {k: 0 for k in sems}
        E = _EvDict()

        def tick(key, instr, k=1):
            instr.then_inc(sems[key], k)
            n[key] += k
            return (key, n[key])

        def ev(key):
            return (key, n[key])

        def wte(eng, evt):
            if evt:
                eng.wait_ge(sems[evt[0]], evt[1])

        CA, CB, CC, CD, CGB, CDSC, CCB, CDLR, CDEFF, CSIN, CUM, CUST = range(12)

        def col(idx, j=None):
            if j is None:
                return cols[:, idx * 16:(idx + 1) * 16]
            return cols[:, idx * 16 + j: idx * 16 + j + 1]

        sections = []

        class _SelfSync:

            def __init__(self, eng, key):
                self._eng = eng
                self._key = key
                self._last = -1

            def wait_ge(self, *a, **k):
                return self._eng.wait_ge(*a, **k)

            def __getattr__(self, name):
                real = getattr(self._eng, name)

                def f(*a, **k):
                    if n[self._key] > 0 and n[self._key] != self._last:
                        self._eng.wait_ge(sems[self._key], n[self._key])
                        self._last = n[self._key]
                    return real(*a, **k)
                return f

        _selfsync_key = {"vector": "dve", "scalar": "act"}

        def sect(engname, fn):
            if engname in _selfsync_key:
                key = _selfsync_key[engname]
                orig = fn

                def wrapped(eng, _orig=orig, _key=key):
                    return _orig(_SelfSync(eng, _key))
                sections.append((engname, wrapped))
            else:
                sections.append((engname, fn))

        NVEC = 14

        def pre_sync(s):
            tick("cst", s.dma_start(out=idf[:], in_=identf[:]), 16)
            tick("cst", s.dma_start(out=idb[:], in_=identb[:]), 16)
            tick("cst", s.dma_start(out=maskc[:], in_=masks[:]), 16)
            tick("cst", s.dma_start(
                out=dlruc.ap().rearrange("p (j r) -> p j r", j=ND),
                in_=dlru.rearrange("(j p) r -> p j r", p=P)), 16)
            tick("cst", s.dma_start(out=dlrvc[:], in_=dlrv[:]), 16)
            E["pre_const"] = ev("cst")
            for i in range(NVEC):
                if i >= 1:
                    wte(s, E[("vtp", i - 1)])
                if i < 6:
                    src = [av, bv, cv, dv, gbv, dscv][i][:]
                elif i < 10:
                    src = gscv[0:1, (i - 6) * D:(i - 5) * D]
                else:
                    src = uscv[0:1, (i - 10) * D:(i - 9) * D]
                E[("vrow", i)] = tick("vrw", s.dma_start(out=vrow[:], in_=src), 16)
        sect("sync", pre_sync)

        def pre_pe(t):
            wte(t, E["pre_const"])
            for i in range(NVEC):
                wte(t, E[("vrow", i)])
                if i >= 2:
                    wte(t, E[("vev", i - 2)])
                for k in range(16):
                    ins = t.transpose(psF[i % 2][:, k:k + 1],
                                      vrow[0:1, k * P:(k + 1) * P],
                                      idf[0:1, 0:1])
                E[("vtp", i)] = tick("pe", ins)
        sect("tensor", pre_pe)

        def pre_dve(v):
            for i in range(NVEC):
                wte(v, E[("vtp", i)])
                if i < 6:
                    dst = col([CA, CB, CC, CD, CGB, CDSC][i])
                elif i < 10:
                    dst = gscc[:, (i - 6) * 16:(i - 5) * 16]
                else:
                    dst = uscc[:, (i - 10) * 16:(i - 9) * 16]
                E[("vev", i)] = tick("dve", v.tensor_copy(dst, psF[i % 2][:, 0:16]))
            tick("dve", v.memset(carr[:], 0.0))
            tick("dve", v.memset(carrh[:, 0:16], 0.0))
            tick("dve", v.memset(onesr[:], 1.0))
            tick("dve", v.memset(epst[:], EPS))
            E["pre_dve"] = tick("dve", v.tensor_tensor(col(CCB), col(CB), col(CC),
                                                       op=ALU.mult))
        sect("vector", pre_dve)

        def pre_act(a):
            wte(a, E[("vev", 0)])
            E["tanh1"] = tick("act", a.activation(lnst[:, 0:16], col(CA), AF.Tanh))
        sect("scalar", pre_act)

        def pre_dve2(v):
            wte(v, E["tanh1"])
            E["aacol"] = tick("dve", v.tensor_copy(col(CA), lnst[:, 0:16]))
        sect("vector", pre_dve2)

        aac = col(CA)

        mats = [
            dict(src=gate_w, scr=wgs, tiles=ND, wid=D, int_=False),
            dict(src=gq, scr=wgfs, tiles=NI, wid=D, int_=True),
            dict(src=uq, scr=wufs, tiles=NI, wid=D, int_=True),
            dict(src=dq, scr=wdfs, tiles=ND, wid=II, int_=True),
        ]
        _sb = 0
        wtiles = []
        for mi, m in enumerate(mats):
            m["segbase"] = _sb
            m["nseg"] = m["wid"] // D
            _sb += m["tiles"] * m["nseg"]
            for j in range(m["tiles"]):
                wtiles.append((mi, j))
        NSEG_ALL = _sb

        wodone = {"s": None}
        wdone = {"s": None}

        def w_stage(gj):
            mi, j = wtiles[gj]
            m = mats[mi]
            if m["wid"] == D:
                slot = gj % 8
                lo = slot * D
            else:
                lo = (gj % 2) * 8192
            stg = hI if m["int_"] else hF
            return stg[:, lo: lo + m["wid"]]

        def w_stage_waits(gj):
            mi, j = wtiles[gj]
            m = mats[mi]
            out = []
            if m["wid"] == D:
                if gj >= 8:
                    out.append(gj - 8)
            else:
                prev = gj - 2
                if prev >= 144:
                    out.append(prev)
                else:
                    base = 0 if gj % 2 == 0 else 4
                    for s2 in range(base, base + 4):
                        cand = [g for g in range(144) if g % 8 == s2]
                        if cand:
                            out.append(max(cand))
            return out

        def w_sync_slice(s, lo, hi):
            if lo == 0:
                wte(s, E["pre_dve"])
            for gj in range(lo, hi):
                mi, j = wtiles[gj]
                m = mats[mi]
                for pg in w_stage_waits(gj):
                    pmi, pj = wtiles[pg]
                    wte(s, E[("wconv", pmi, pj, mats[pmi]["nseg"] - 1)])
                E[("win", mi, j)] = tick(
                    "wi%d" % (gj % 8),
                    s.dma_start(out=w_stage(gj), in_=m["src"][j * P:(j + 1) * P, :]),
                    16)

        def w_pl_slice(p, lo, hi):
            def wout(si):
                if si < 0 or si in wodone["s"]:
                    return
                wodone["s"].add(si)
                for mi2, m2 in enumerate(mats):
                    if m2["segbase"] <= si < m2["segbase"] + m2["tiles"] * m2["nseg"]:
                        rel = si - m2["segbase"]
                        j2, seg2 = rel // m2["nseg"], rel % m2["nseg"]
                        break
                pb = wstr_d[si % 2][:, 0:D]
                for g in range(4):
                    wte(p, E[("wev", si * 4 + g)])
                E[("wout", si)] = tick(
                    "wo%d" % (si % 2),
                    p.dma_start(out=m2["scr"][j2][:, seg2 * D:(seg2 + 1) * D],
                                in_=pb), 16)

            last_si = -1
            for gj in range(lo, hi):
                mi, j = wtiles[gj]
                m = mats[mi]
                stage = w_stage(gj)
                for seg in range(m["nseg"]):
                    si = m["segbase"] + j * m["nseg"] + seg
                    wte(p, E[("win", mi, j)])
                    if si >= 2:
                        wte(p, E[("wtpl", si - 2)])
                    srcap = stage[:, seg * D:(seg + 1) * D]
                    cbuf = si % 2
                    dst = convb[:, cbuf * D:(cbuf + 1) * D]
                    if m["int_"]:
                        ins = p.tensor_scalar(dst, srcap, -128.0, None, op0=ALU.add)
                    else:
                        ins = p.tensor_copy(dst, srcap)
                    E[("wconv", mi, j, seg)] = tick("plc", ins)
                    wout(si - 2)
                    last_si = si
            wout(last_si - 1)
            wout(last_si)
            if hi == len(wtiles):
                E["W_plc"] = ev("plc")
                E["W_spl0"] = ev("wo0")
                E["W_spl1"] = ev("wo1")

        def w_dve_slice(v, lo, hi):
            for gj in range(lo, hi):
                mi, j = wtiles[gj]
                m = mats[mi]
                for seg in range(m["nseg"]):
                    si = m["segbase"] + j * m["nseg"] + seg
                    if si in wdone["s"]:
                        continue
                    wdone["s"].add(si)
                    pb = wstr_d[si % 2][:, 0:D]
                    for g in range(4):
                        gi = si * 4 + g
                        if gi % 2 != 0:
                            continue
                        wte(v, E[("wtp", gi)])
                        if si >= 2:
                            wte(v, E[("wout", si - 2)])
                        E[("wev", gi)] = tick("dve", v.tensor_copy(
                            pb[:, g * TC:(g + 1) * TC], psT[gi % 2][:, 0:TC]))
            if hi == len(wtiles):
                E["W_dve"] = ev("dve")

        def w_pe_slice(t, lo, hi):
            for gj in range(lo, hi):
                mi, j = wtiles[gj]
                m = mats[mi]
                for seg in range(m["nseg"]):
                    si = m["segbase"] + j * m["nseg"] + seg
                    cbuf = si % 2
                    wte(t, E[("wconv", mi, j, seg)])
                    for g in range(4):
                        gi = si * 4 + g
                        if gi >= 2:
                            wte(t, E[("wev", gi - 2)])
                        for q in range(4):
                            kb = g * 4 + q
                            ins = t.transpose(
                                psT[gi % 2][:, q * P:(q + 1) * P],
                                convb[:, cbuf * D + kb * P: cbuf * D + (kb + 1) * P],
                                idb[:])
                        E[("wtp", gi)] = tick("pe", ins)
                    E[("wtpl", si)] = ev("pe")
            if hi == len(wtiles):
                E["W_pe"] = ev("pe")

        def w_act_slice(a, lo, hi):
            for gj in range(lo, hi):
                mi, j = wtiles[gj]
                m = mats[mi]
                for seg in range(m["nseg"]):
                    si = m["segbase"] + j * m["nseg"] + seg
                    pb = wstr_d[si % 2][:, 0:D]
                    for g in range(4):
                        gi = si * 4 + g
                        if gi % 2 != 1:
                            continue
                        wte(a, E[("wtp", gi)])
                        if si >= 2:
                            wte(a, E[("wout", si - 2)])
                        E[("wev", gi)] = tick("act", a.activation(
                            pb[:, g * TC:(g + 1) * TC], psT[gi % 2][:, 0:TC],
                            AF.Copy))
            if hi == len(wtiles):
                E["W_act"] = ev("act")

        def emit_w_slice(lo, hi):
            if lo >= hi:
                return
            sect("sync", lambda s, lo=lo, hi=hi: w_sync_slice(s, lo, hi))
            sect("gpsimd", lambda p, lo=lo, hi=hi: w_pl_slice(p, lo, hi))
            sect("vector", lambda v, lo=lo, hi=hi: w_dve_slice(v, lo, hi))
            sect("tensor", lambda t, lo=lo, hi=hi: w_pe_slice(t, lo, hi))
            sect("scalar", lambda a, lo=lo, hi=hi: w_act_slice(a, lo, hi))

        emit_w_slice(0, ND)

        for c in range(NCH):
            def a_sync(s, c=c):
                for sft in range(NT):
                    g = c * NT + sft
                    if g >= 2:
                        wte(s, E[("a_xn", g - 2)])
                        wte(s, E[("a_sq", g - 2)])
                    E[("a_xl", g)] = tick("xl%d" % (sft % 2), s.dma_start(
                        out=x_rot[sft % 2][:],
                        in_=x_in[c * TC + sft * P: c * TC + (sft + 1) * P, :]), 16)
                for j in range(ND):
                    jj = c * ND + j
                    if jj >= 2:
                        wte(s, E[("a_gg", (jj - 2) // ND, (jj - 2) % ND)])
                    if c == 0:
                        wte(s, E[("wout", j)])
                    E[("a_wg", c, j)] = tick(
                        "wg%d" % (j % 2),
                        s.dma_start(out=wstr_g[j % 2][:], in_=wgs[j]), 16)
            sect("sync", a_sync)

            def a_act(a, c=c):
                for half in range(2):
                    for sft in (2 * half, 2 * half + 1):
                        g = c * NT + sft
                        wte(a, E[("a_xl", g)])
                        if g >= 2:
                            wte(a, E[("a_tpl", g - 2)])
                        E[("a_sq", g)] = tick("act", a.activation(
                            tok_rot[sft % 2][:], x_rot[sft % 2][:],
                            AF.Square, accum_out=lnst[:, sft: sft + 1]))
                    for sft in (2 * half, 2 * half + 1):
                        g = c * NT + sft
                        wte(a, E[("a_var", g)])
                        E[("a_sd", g)] = tick("act", a.activation(
                            lnst[:, 8 + sft: 8 + sft + 1],
                            lnst[:, 4 + sft: 4 + sft + 1], AF.Sqrt,
                            bias=epst[:, 0:1]))
                for j in range(ND):
                    jj = c * ND + j
                    wte(a, E[("a_gg", c, j)])
                    if jj >= 2:
                        wte(a, E[("a_ug", (jj - 2) // ND, (jj - 2) % ND)])
                    E[("a_sig", c, j)] = tick("act", a.activation(
                        sg_rot[j % 2][:], psF[j % 4][:], AF.Sigmoid,
                        bias=col(CGB, j)))
            sect("scalar", a_act)

            def a_pe(t, c=c):
                for sft in range(NT):
                    g = c * NT + sft
                    wte(t, E[("a_xn", g)])
                    for grp in range(4):
                        if sft == 0:
                            if c > 0:
                                wte(t, E[("a_sig", c - 1, 12 + grp)])
                        else:
                            wte(t, E[("a_tev", g - 1, grp)])
                        for q in range(4):
                            kb = grp * 4 + q
                            ins = t.transpose(
                                psF[grp][:, q * P:(q + 1) * P],
                                tok_rot[sft % 2][:, kb * P:(kb + 1) * P],
                                idf[:])
                        E[("a_tp", g, grp)] = tick("pe", ins)
                    E[("a_tpl", g)] = ev("pe")
                wte(t, E[("a_tevl", c)])
                for j in range(ND):
                    jj = c * ND + j
                    wte(t, E[("a_wg", c, j)])
                    if j < 4:
                        wte(t, E[("a_tev", c * NT + 3, j % 4)])
                    if jj >= 4:
                        wte(t, E[("a_sig", (jj - 4) // ND, (jj - 4) % ND)])
                    for k in range(ND):
                        ins = t.matmul(psF[j % 4][:],
                                       wstr_g[j % 2][:, k * P:(k + 1) * P],
                                       xnT[:, k * TC:(k + 1) * TC],
                                       start=(k == 0), stop=(k == ND - 1))
                    E[("a_gg", c, j)] = tick("pe", ins)
            sect("tensor", a_pe)

            def a_dve(v, c=c):
                def emit_evacs(v, c, sft):
                    g = c * NT + sft
                    for grp in range(4):
                        wte(v, E[("a_tp", g, grp)])
                        if g == 0 and grp == 0:
                            pass
                        if c > 0 and sft == 0 and grp == 0:
                            wte(v, E[("a_gg", c - 1, ND - 1)])
                            wte(v, E[("a_xnst", c - 1)])
                        dst = xnT.ap().rearrange("p (k t) -> p k t", k=ND)[
                            :, grp * 4:(grp + 1) * 4, sft * P:(sft + 1) * P]
                        src = psF[grp].ap().rearrange(
                            "p (q u) -> p q u", q=4)
                        E[("a_tev", g, grp)] = tick("dve", v.tensor_copy(dst, src))

                for sft in range(NT):
                    g = c * NT + sft
                    wte(v, E[("a_xl", g)])
                    tick("dve", v.reduce_sum(lnst[:, 12 + sft: 12 + sft + 1],
                                             x_rot[sft % 2][:], axis=AX))
                    tick("dve", v.tensor_scalar(
                        lnst[:, 12 + sft: 12 + sft + 1],
                        lnst[:, 12 + sft: 12 + sft + 1], 1.0 / D, None,
                        op0=ALU.mult))
                    wte(v, E[("a_sq", g)])
                    tick("dve", v.tensor_scalar(
                        lnst[:, 4 + sft: 4 + sft + 1], lnst[:, sft: sft + 1],
                        1.0 / D, None, op0=ALU.mult))
                    tick("dve", v.tensor_tensor(
                        jk16[:, 0:1], lnst[:, 12 + sft:12 + sft + 1],
                        lnst[:, 12 + sft:12 + sft + 1], op=ALU.mult))
                    E[("a_var", g)] = tick("dve", v.tensor_tensor(
                        lnst[:, 4 + sft:4 + sft + 1],
                        lnst[:, 4 + sft:4 + sft + 1], jk16[:, 0:1],
                        op=ALU.subtract))
                    wte(v, E[("a_sd", g)])
                    tick("dve", v.reciprocal(lnst[:, 8 + sft:8 + sft + 1],
                                             lnst[:, 8 + sft:8 + sft + 1]))
                    if g >= 2:
                        wte(v, E[("a_tpl", g - 2)])
                    E[("a_xn", g)] = tick("dve", v.tensor_scalar(
                        tok_rot[sft % 2][:], x_rot[sft % 2][:],
                        lnst[:, 12 + sft:12 + sft + 1],
                        lnst[:, 8 + sft:8 + sft + 1],
                        op0=ALU.subtract, op1=ALU.mult))
                    if sft >= 1:
                        emit_evacs(v, c, sft - 1)
                emit_evacs(v, c, NT - 1)
                E[("a_tevl", c)] = ev("dve")
                for j in range(ND):
                    wte(v, E[("a_sig", c, j)])
                    if c > 0 and j == 0:
                        wte(v, E[("a_ugst", c - 1)])
                    E[("a_ug", c, j)] = tick("dve", v.scalar_tensor_tensor(
                        ug[:, j * TC:(j + 1) * TC],
                        xnT[:, j * TC:(j + 1) * TC], 1.0, sg_rot[j % 2][:],
                        op0=ALU.mult, op1=ALU.mult,
                        accum_out=usall[:, j * 8 + c: j * 8 + c + 1]))
                E[("a_ugl", c)] = ev("dve")
                wte(v, ev("dve"))
                for j in range(ND):
                    tick("dve", v.tensor_tensor_scan(
                        sc_rot[j % 2][:], aac[:, j:j + 1].broadcast_to([P, TC]),
                        ug[:, j * TC:(j + 1) * TC], carr[:, j:j + 1],
                        op0=ALU.mult, op1=ALU.add))
                    wte(v, ev("dve"))
                    tick("dve", v.tensor_copy(carr[:, j:j + 1],
                                              sc_rot[j % 2][:, TC - 1:TC]))
                E[("a_carr", c)] = tick("dve", v.tensor_copy(
                    carrh[:, (c + 1) * 16:(c + 2) * 16], carr[:]))
            sect("vector", a_dve)

            def a_spl(p, c=c):
                wte(p, E[("a_tevl", c)])
                E[("a_xnst", c)] = tick("xst",
                                        p.dma_start(out=xnts[c], in_=xnT[:]), 16)
                wte(p, E[("a_ugl", c)])
                E[("a_ugst", c)] = tick("ust",
                                        p.dma_start(out=ugs[c], in_=ug[:]), 16)
            sect("gpsimd", a_spl)

            emit_w_slice(ND + 18 * c, min(ND + 18 * (c + 1), len(wtiles)))

        def b_dve1(v):
            for j in range(ND):
                tick("dve", v.reduce_sum(col(CUST, j),
                                         usall[:, j * 8:(j + 1) * 8], axis=AX))
            E["b_ust"] = ev("dve")
        sect("vector", b_dve1)

        def b_pe1(t):
            wte(t, E[("a_carr", NCH - 1)])
            wte(t, E["b_ust"])
            wte(t, E[("a_sig", NCH - 1, 12)])
            wte(t, E[("a_sig", NCH - 1, 13)])
            ins = t.transpose(psF[0][0:16, 0:P], carrh[:, 8 * 16: 9 * 16], idf[:])
            E["b_ctp"] = tick("pe", ins)
            ins = t.transpose(psF[1][0:16, 0:P], col(CUST), idf[:])
            E["b_utp"] = tick("pe", ins)
        sect("tensor", b_pe1)

        def b_dve2(v):
            wte(v, E["b_ctp"])
            tick("dve", v.tensor_copy(carrT[:], psF[0][0:16, 0:P]))
            wte(v, E["b_utp"])
            tick("dve", v.tensor_copy(ustotT[:], psF[1][0:16, 0:P]))
            E["b_rows"] = ev("dve")
        sect("vector", b_dve2)

        def b_sync1(s):
            wte(s, E["b_rows"])
            tick("cci", s.dma_start(
                out=ccin_ap[0:1, :].rearrange("o (j p) -> j (o p)", p=P),
                in_=carrT[:]), 16)
            tick("cci", s.dma_start(
                out=ccin_ap[1:2, :].rearrange("o (j p) -> j (o p)", p=P),
                in_=ustotT[:]), 16)
            E["b_ccin"] = ev("cci")
        sect("sync", b_sync1)

        def b_cc(p):
            wte(p, E["b_ccin"])
            tick("cc", p.collective_compute(
                "AllGather", ALU.bypass,
                replica_groups=[list(range(8))],
                ins=[ccin_ap.opt()],
                outs=[ccout_ap.opt()]), 1)
        sect("gpsimd", b_cc)

        def b_sync2(s):
            s.wait_ge(sems["cc"], 1)
            E["b_g"] = tick("ccg", s.dma_start(out=gcol, in_=ccout_ap[:, :]), 16)
        sect("sync", b_sync2)

        def b_pe2(t):
            wte(t, E["b_g"])
            wte(t, E[("a_sig", NCH - 1, 14)])
            wte(t, E[("a_sig", NCH - 1, 15)])
            for j in range(ND):
                ins = t.transpose(psF[2][:, j * 16:(j + 1) * 16],
                                  gcol[:, j * P:(j + 1) * P], idf[0:16, 0:16])
                tick("pe", ins)
            E["b_gtp"] = ev("pe")
            for m in range(4):
                ins = t.matmul(psF[3][:, m * 16:(m + 1) * 16], onesr[:],
                               maskc[0:1, m * 16:(m + 1) * 16],
                               start=True, stop=True)
                tick("pe", ins)
            E["b_mbc"] = ev("pe")
        sect("tensor", b_pe2)

        def b_dve3(v):
            wte(v, E["b_gtp"])
            tick("dve", v.tensor_copy(hF[:, 0:256], psF[2][:, 0:256]))
            wte(v, E["b_mbc"])
            tick("dve", v.tensor_copy(mbc[:], psF[3][:, 0:64]))
            E["b_cols"] = ev("dve")
            for m in range(4):
                for j in range(ND):
                    tick("dve", v.scalar_tensor_tensor(
                        jk16[:], hF[:, j * 16:(j + 1) * 16], 1.0,
                        mbc[:, m * 16:(m + 1) * 16],
                        op0=ALU.mult, op1=ALU.mult,
                        accum_out=qm[:, m * 16 + j: m * 16 + j + 1]))
            tick("dve", v.tensor_copy(pwa[:], aac))
            cur, nxt = pwa, pwb
            for step in range(12):
                wte(v, ev("dve"))
                tick("dve", v.tensor_tensor(nxt[:], cur[:], cur[:], op=ALU.mult))
                if step == 8:
                    tick("dve", v.tensor_copy(a512[:], nxt[:]))
                cur, nxt = nxt, cur
            tick("dve", v.tensor_copy(pw1[:], cur[:]))
            wte(v, ev("dve"))
            tick("dve", v.tensor_tensor(pw2[:], pw1[:], pw1[:], op=ALU.mult))
            wte(v, ev("dve"))
            tick("dve", v.tensor_tensor(jk16[:], pw1[:], qm[:, 16:32], op=ALU.mult))
            wte(v, ev("dve"))
            tick("dve", v.tensor_tensor(sin_t[:], qm[:, 0:16], jk16[:], op=ALU.add))
            wte(v, ev("dve"))
            tick("dve", v.tensor_tensor(jk16[:], pw2[:], qm[:, 32:48], op=ALU.mult))
            wte(v, ev("dve"))
            tick("dve", v.tensor_tensor(sin_t[:], sin_t[:], jk16[:], op=ALU.add))
            tick("dve", v.tensor_scalar(col(CUM), qm[:, 48:64], 1.0 / 16384.0,
                                        None, op0=ALU.mult))
            E["b_umean"] = ev("dve")
        sect("vector", b_dve3)

        def b_pe3(t):
            wte(t, E["b_umean"])
            for k in range(ND):
                ins = t.matmul(psF[4][0:4, 0:1], dlruc[:, k * 4:(k + 1) * 4],
                               col(CUM, k), start=(k == 0), stop=(k == ND - 1))
            E["b_r"] = tick("pe", ins)
        sect("tensor", b_pe3)

        def b_dve4(v):
            wte(v, E["b_r"])
            E["b_rs"] = tick("dve", v.tensor_copy(rsb[:], psF[4][0:4, 0:1]))
        sect("vector", b_dve4)

        def b_pe4(t):
            wte(t, E["b_rs"])
            for j in range(ND):
                ins = t.matmul(psF[5][:, j:j + 1],
                               dlrvc[0:4, j * P:(j + 1) * P], rsb[:],
                               start=True, stop=True)
                tick("pe", ins)
            E["b_dlr"] = ev("pe")
        sect("tensor", b_pe4)

        def b_dve5(v):
            wte(v, E["b_dlr"])
            tick("dve", v.tensor_scalar(col(CDLR), psF[5][:, 0:16], 0.25, None,
                                        op0=ALU.mult))
            wte(v, ev("dve"))
            tick("dve", v.tensor_tensor(col(CDEFF), col(CD), col(CDLR),
                                        op=ALU.add))
            tick("dve", v.tensor_copy(cura[:], sin_t[:]))
            cur, nxt = cura, curb
            for c in range(NCH):
                wte(v, ev("dve"))
                tick("dve", v.tensor_tensor(sf[:, c * 16:(c + 1) * 16],
                                            carrh[:, c * 16:(c + 1) * 16],
                                            cur[:], op=ALU.add))
                tick("dve", v.tensor_tensor(nxt[:], cur[:], a512[:], op=ALU.mult))
                cur, nxt = nxt, cur
            E["b_done"] = ev("dve")
        sect("vector", b_dve5)

        for c in range(NCH):
            def c_sync(s, c=c):
                if c == 0:
                    wte(s, E[("a_gg", NCH - 1, ND - 1)])
                    wte(s, E[("a_xnst", NCH - 1)])
                    wte(s, E["W_spl0"])
                    wte(s, E["W_spl1"])
                else:
                    wte(s, E[("c_gu", c - 1, NI - 1)])
                    wte(s, E[("c_dtp", c - 1, NT - 1, 3)])
                E[("c_xnl", c)] = tick("xnl",
                                       s.dma_start(out=xnT[:], in_=xnts[c]), 16)
                if c == 0:
                    wte(s, E[("a_carr", NCH - 1)])
                    wte(s, E[("a_ugst", NCH - 1)])
                else:
                    wte(s, E[("c_dtp", c - 1, NT - 1, 3)])
                E[("c_ugl", c)] = tick("ugl",
                                       s.dma_start(out=ug[:], in_=ugs[c]), 16)
                for j in range(NI):
                    jj = c * NI + j
                    if jj >= 2:
                        wte(s, E[("c_gu", (jj - 2) // NI, (jj - 2) % NI)])
                    E[("c_wgl", c, j)] = tick(
                        "wg%d" % (j % 2),
                        s.dma_start(out=wstr_g[j % 2][:], in_=wgfs[j]), 16)
                    E[("c_wul", c, j)] = tick(
                        "wu%d" % (j % 2),
                        s.dma_start(out=wstr_u[j % 2][:], in_=wufs[j]), 16)
                for j2 in range(ND):
                    jj2 = c * ND + j2
                    for hf in range(2):
                        if jj2 >= 1:
                            wte(s, E[("c_dg", (jj2 - 1) // ND, (jj2 - 1) % ND)])
                        hb = (j2 * 2 + hf) % 2
                        E[("c_wdl", c, j2, hf)] = tick(
                            "wd%d" % hb,
                            s.dma_start(out=wstr_d[hb][:],
                                        in_=wdfs[j2][:, hf * 4096:(hf + 1) * 4096]),
                            16)
            sect("sync", c_sync)

            def c_pe(t, c=c):
                wte(t, E[("c_xnl", c)])
                if c == 0:
                    wte(t, E["b_done"])
                for j in range(NI):
                    gb = (j % 2) * 2
                    ub = gb + 1
                    wte(t, E[("c_wgl", c, j)])
                    if j >= 2:
                        wte(t, E[("c_sg", c, j - 2)])
                    elif c > 0:
                        wte(t, E[("c_oev4", c - 1, NT - 1, gb)])
                    for k in range(ND):
                        ins = t.matmul(psF[gb][:],
                                       wstr_g[j % 2][:, k * P:(k + 1) * P],
                                       xnT[:, k * TC:(k + 1) * TC],
                                       start=(k == 0), stop=(k == ND - 1))
                    E[("c_gmm", c, j)] = tick("pe", ins)
                    wte(t, E[("c_wul", c, j)])
                    if j >= 2:
                        wte(t, E[("c_ut", c, j - 2)])
                    elif c > 0:
                        wte(t, E[("c_oev4", c - 1, NT - 1, ub)])
                    for k in range(ND):
                        ins = t.matmul(psF[ub][:],
                                       wstr_u[j % 2][:, k * P:(k + 1) * P],
                                       xnT[:, k * TC:(k + 1) * TC],
                                       start=(k == 0), stop=(k == ND - 1))
                    tick("pe", ins)
                    E[("c_umm", c, j)] = ("pe", n["pe"])
                    E[("c_gu", c, j)] = ("pe", n["pe"])
                for j2 in range(ND):
                    jj2 = c * ND + j2
                    db = 4 + j2 % 2
                    if j2 == 0:
                        wte(t, E[("c_h", c, NI - 1)])
                    if jj2 >= 2:
                        wte(t, E[("c_gf", (jj2 - 2) // ND, (jj2 - 2) % ND)])
                    for hf in range(2):
                        wte(t, E[("c_wdl", c, j2, hf)])
                        for kk in range(32):
                            k = hf * 32 + kk
                            ins = t.matmul(
                                psF[db][:],
                                wstr_d[(j2 * 2 + hf) % 2][:, kk * P:(kk + 1) * P],
                                h[:, k * TC:(k + 1) * TC],
                                start=(k == 0), stop=(k == NI - 1))
                    E[("c_dg", c, j2)] = tick("pe", ins)
                for sft in range(NT):
                    if sft == 0:
                        wte(t, E[("c_dl", c, ND - 1)])
                    for grp in range(4):
                        if sft == 0:
                            if grp in (0, 2):
                                wte(t, E[("c_sg", c, NI - 2 + (grp // 2))])
                            else:
                                wte(t, E[("c_ut", c, NI - 2 + ((grp - 1) // 2))])
                        else:
                            wte(t, E[("c_oev4", c, sft - 1, grp)])
                        for q in range(4):
                            j2 = grp * 4 + q
                            ins = t.transpose(
                                psF[grp][:, q * P:(q + 1) * P],
                                delta_ap(j2, sft * P, P),
                                idf[:])
                        E[("c_dtp", c, sft, grp)] = tick("pe", ins)
            sect("tensor", c_pe)

            def c_act(a, c=c):
                for j in range(NI):
                    jj = c * NI + j
                    wte(a, E[("c_gmm", c, j)])
                    if jj >= 2:
                        jp = jj - 2
                        wte(a, E[("c_h", jp // NI, jp % NI)])
                    else:
                        wte(a, E[("a_ug", NCH - 1, ND - 2 + j)])
                    E[("c_sg", c, j)] = tick("act", a.activation(
                        sg_rot[j % 2][:], psF[(j % 2) * 2][:], AF.Silu,
                        scale=gscc[:, j:j + 1]))
                for j2 in range(ND):
                    jj2 = c * ND + j2
                    wte(a, E[("c_dg", c, j2)])
                    if jj2 >= 2:
                        wte(a, E[("c_dl", (jj2 - 2) // ND, (jj2 - 2) % ND)])
                    E[("c_gf", c, j2)] = tick("act", a.activation(
                        gf_rot[j2 % 2][:], psF[4 + j2 % 2][:], AF.Silu,
                        scale=col(CDSC, j2)))
            sect("scalar", c_act)

            def c_dve(v, c=c):
                if c == 0:
                    wte(v, E["b_done"])
                for j in range(NI):
                    wte(v, E[("c_umm", c, j)])
                    E[("c_ut", c, j)] = tick("dve", v.tensor_scalar(
                        ut_rot[j % 2][:], psF[(j % 2) * 2 + 1][:],
                        uscc[:, j:j + 1], None, op0=ALU.mult))
                    wte(v, E[("c_sg", c, j)])
                    if j == 0 and c > 0:
                        wte(v, E[("c_dg", c - 1, ND - 1)])
                    E[("c_h", c, j)] = tick("dve", v.tensor_tensor(
                        h[:, j * TC:(j + 1) * TC], sg_rot[j % 2][:],
                        ut_rot[j % 2][:], op=ALU.mult))
                wte(v, E[("c_ugl", c)])
                for j2 in range(ND):
                    wte(v, ev("dve"))
                    tick("dve", v.tensor_tensor_scan(
                        sc_rot[j2 % 2][:],
                        aac[:, j2:j2 + 1].broadcast_to([P, TC]),
                        ug[:, j2 * TC:(j2 + 1) * TC],
                        sf[:, c * 16 + j2: c * 16 + j2 + 1],
                        op0=ALU.mult, op1=ALU.add))
                    wte(v, ev("dve"))
                    tick("dve", v.tensor_scalar(
                        y_rot[j2 % 2][:], sc_rot[j2 % 2][:], col(CCB, j2),
                        None, op0=ALU.mult))
                    E[("c_y", c, j2)] = tick("dve", v.scalar_tensor_tensor(
                        sc_rot[j2 % 2][:], ug[:, j2 * TC:(j2 + 1) * TC],
                        col(CDEFF, j2), y_rot[j2 % 2][:],
                        op0=ALU.mult, op1=ALU.add))
                    wte(v, E[("c_gf", c, j2)])
                    E[("c_dl", c, j2)] = tick("dve", v.tensor_tensor(
                        delta_ap(j2, 0, TC), sc_rot[j2 % 2][:],
                        gf_rot[j2 % 2][:], op=ALU.mult))
                for sft in range(NT):
                    g = c * NT + sft
                    wte(v, E[("c_xl", g)])
                    if E.get(("c_ost", g - 2)):
                        wte(v, E[("c_ost", g - 2)])
                    for grp in range(4):
                        wte(v, E[("c_dtp", c, sft, grp)])
                        E[("c_oev4", c, sft, grp)] = tick("dve", v.tensor_tensor(
                            tok_rot[sft % 2][:, grp * TC:(grp + 1) * TC],
                            psF[grp][:],
                            x_rot[sft % 2][:, grp * TC:(grp + 1) * TC],
                            op=ALU.add))
                    E[("c_oev", g)] = ev("dve")
            sect("vector", c_dve)

            def c_spl(p, c=c):
                for sft in range(NT):
                    g = c * NT + sft
                    if c == 0 and sft < 2:
                        wte(p, E[("a_xn", (NCH - 1) * NT + 2 + sft)])
                        wte(p, E[("a_sq", (NCH - 1) * NT + 2 + sft)])
                    if E.get(("c_oev", g - 2)):
                        wte(p, E[("c_oev", g - 2)])
                    E[("c_xl", g)] = tick("cx%d" % (sft % 2), p.dma_start(
                        out=x_rot[sft % 2][:],
                        in_=x_in[c * TC + sft * P: c * TC + (sft + 1) * P, :]), 16)
                for sft in range(NT):
                    g = c * NT + sft
                    wte(p, E[("c_oev", g)])
                    E[("c_ost", g)] = tick("os%d" % (sft % 2), p.dma_start(
                        out=out_d[c * TC + sft * P: c * TC + (sft + 1) * P, :],
                        in_=tok_rot[sft % 2][:]), 16)
            sect("gpsimd", c_spl)

        def tail(p):
            for k in ("os0", "os1", "xst", "ust", "wo0", "wo1"):
                p.wait_ge(sems[k], n[k])
        sect("gpsimd", tail)

        stub = _StubEngine()
        wdone["s"] = set()
        wodone["s"] = set()
        for engname, fn in sections:
            fn(stub)
        pass1 = dict(n)
        for k in n:
            n[k] = 0
        wdone["s"] = set()
        wodone["s"] = set()
        E.pass2 = True
        for engname, fn in sections:
            getattr(block, engname)(fn)
        assert n == pass1, f"two-pass mismatch: {n} vs {pass1}"

    return nc


_NC_CACHE = None


def kernel(**inputs):
    global _NC_CACHE
    x = np.ascontiguousarray(inputs["L0_attn_u"])
    B, T, Dm = x.shape
    assert (B, T, Dm) == (2, 16384, 2048)

    def row(v):
        return np.ascontiguousarray(np.asarray(v, dtype=np.float32).reshape(1, -1))

    shared = dict(
        av=row(inputs["L0_attn_a"]),
        bv=row(inputs["L0_attn_b"]),
        cv=row(inputs["L0_attn_c"]),
        dv=row(inputs["L0_attn_d"]),
        gbv=row(inputs["L0_attn_gate_b"]),
        dscv=row(inputs["L0_ffn_down_scale"]),
        gscv=row(inputs["L0_ffn_gate_scale"]),
        uscv=row(inputs["L0_ffn_up_scale"]),
        gate_w=np.ascontiguousarray(inputs["L0_attn_gate_w"], dtype=np.float32),
        gq=np.ascontiguousarray(inputs["L0_ffn_gate_w"], dtype=np.int32),
        uq=np.ascontiguousarray(inputs["L0_ffn_up_w"], dtype=np.int32),
        dq=np.ascontiguousarray(inputs["L0_ffn_down_w"], dtype=np.int32),
        dlru=np.ascontiguousarray(inputs["L0_attn_d_lr_u"], dtype=np.float32),
        dlrv=np.ascontiguousarray(inputs["L0_attn_d_lr_v"], dtype=np.float32),
        identf=np.eye(P, dtype=np.float32),
        identb=np.eye(P, dtype=np.float32).astype(ml_dtypes.bfloat16),
    )

    in_maps = []
    for core in range(8):
        b, q = core // 4, core % 4
        msk = np.zeros((1, 64), dtype=np.float32)
        for j in range(4):
            m = q - 1 - j
            if 0 <= m <= 2:
                msk[0, m * 16 + 2 * (4 * b + j)] = 1.0
            msk[0, 48 + 2 * (4 * b + j) + 1] = 1.0
        im = dict(shared)
        im["x"] = np.ascontiguousarray(x[b, q * TCORE:(q + 1) * TCORE, :])
        im["masks"] = msk
        in_maps.append(im)

    if _NC_CACHE is None:
        _NC_CACHE = build_nc()
    res = run_bass_kernel_spmd(_NC_CACHE, in_maps, core_ids=list(range(8)))

    out = np.empty((B, T, Dm), dtype=np.float32)
    for core in range(8):
        b, q = core // 4, core % 4
        out[b, q * TCORE:(q + 1) * TCORE, :] = res.results[core]["out"]
    return out


# version 09
# speedup vs baseline: 1.4678x; 1.4678x over previous
import numpy as np
import ml_dtypes
import concourse.bass as bass
import concourse.mybir as mybir
from concourse.bass_utils import run_bass_kernel_spmd
from contextlib import ExitStack

F32 = mybir.dt.float32
BF16 = mybir.dt.bfloat16
I32 = mybir.dt.int32
AF = mybir.ActivationFunctionType
ALU = mybir.AluOpType
AX = mybir.AxisListType.X

P = 128
D = 2048
ND = 16
II = 8192
NI = 64
TCORE = 4096
TC = 512
NCH = 8
NT = 4
EPS = 1e-5


class _StubInstr:
    def then_inc(self, *a, **k):
        return self


class _StubEngine:
    def __getattr__(self, name):
        def f(*a, **k):
            return _StubInstr()
        return f


class _EvDict(dict):
    pass2 = False

    def __missing__(self, key):
        if self.pass2:
            raise KeyError(key)
        return 0


def build_nc():
    nc = bass.Bass("TRN2", target_bir_lowering=False)

    x_in = nc.dram_tensor("x", [TCORE, D], F32, kind="ExternalInput").ap()
    av = nc.dram_tensor("av", [1, D], F32, kind="ExternalInput").ap()
    bv = nc.dram_tensor("bv", [1, D], F32, kind="ExternalInput").ap()
    cv = nc.dram_tensor("cv", [1, D], F32, kind="ExternalInput").ap()
    dv = nc.dram_tensor("dv", [1, D], F32, kind="ExternalInput").ap()
    gbv = nc.dram_tensor("gbv", [1, D], F32, kind="ExternalInput").ap()
    dscv = nc.dram_tensor("dscv", [1, D], F32, kind="ExternalInput").ap()
    gscv = nc.dram_tensor("gscv", [1, II], F32, kind="ExternalInput").ap()
    uscv = nc.dram_tensor("uscv", [1, II], F32, kind="ExternalInput").ap()
    gate_w = nc.dram_tensor("gate_w", [D, D], F32, kind="ExternalInput").ap()
    gq = nc.dram_tensor("gq", [II, D], I32, kind="ExternalInput").ap()
    uq = nc.dram_tensor("uq", [II, D], I32, kind="ExternalInput").ap()
    dq = nc.dram_tensor("dq", [D, II], I32, kind="ExternalInput").ap()
    dlru = nc.dram_tensor("dlru", [D, 4], F32, kind="ExternalInput").ap()
    dlrv = nc.dram_tensor("dlrv", [4, D], F32, kind="ExternalInput").ap()
    identf = nc.dram_tensor("identf", [P, P], F32, kind="ExternalInput").ap()
    identb = nc.dram_tensor("identb", [P, P], BF16, kind="ExternalInput").ap()
    masks = nc.dram_tensor("masks", [1, 64], F32, kind="ExternalInput").ap()

    out_d = nc.dram_tensor("out", [TCORE, D], F32, kind="ExternalOutput").ap()

    wgs = nc.dram_tensor("wgs", [ND, P, D], BF16).ap()
    wgfs = nc.dram_tensor("wgfs", [NI, P, D], BF16).ap()
    wufs = nc.dram_tensor("wufs", [NI, P, D], BF16).ap()
    wdfs = nc.dram_tensor("wdfs", [ND, P, II], BF16).ap()
    xnts = nc.dram_tensor("xnts", [NCH, P, ND * TC], BF16).ap()
    ugs = nc.dram_tensor("ugs", [NCH, P, ND * TC], BF16).ap()
    ccin_ap = nc.dram_tensor("ccin", [2, D], F32).ap()
    ccout_ap = nc.dram_tensor("ccout", [16, D], F32).ap()

    with ExitStack() as ctx:
        e = ctx.enter_context

        h = e(nc.sbuf_tensor("h", [P, NI * TC], BF16))
        hF = h.bitcast(F32)
        hI = h.bitcast(I32)
        ug = e(nc.sbuf_tensor("ug", [P, ND * TC], BF16))
        ugF = ug.bitcast(F32)
        xnT = e(nc.sbuf_tensor("xnT", [P, ND * TC], BF16))
        xnTF = xnT.bitcast(F32)

        def delta_ap(j2, lo, width):
            reg = xnTF if j2 % 2 == 0 else ugF
            base = (j2 // 2) * TC
            return reg[:, base + lo: base + lo + width]
        x_rot = [e(nc.sbuf_tensor(f"xr{i}", [P, D], F32)) for i in range(2)]
        tok_rot = [e(nc.sbuf_tensor(f"tr{i}", [P, D], F32)) for i in range(2)]
        wstr_g = [e(nc.sbuf_tensor(f"wg{i}", [P, D], BF16)) for i in range(2)]
        wstr_u = [e(nc.sbuf_tensor(f"wu{i}", [P, D], BF16)) for i in range(2)]
        wstr_d = [e(nc.sbuf_tensor(f"wd{i}", [P, 4096], BF16)) for i in range(2)]
        sg_rot = [e(nc.sbuf_tensor(f"sg{i}", [P, TC], F32)) for i in range(2)]
        ut_rot = [e(nc.sbuf_tensor(f"ut{i}", [P, TC], F32)) for i in range(2)]
        sc_rot = [e(nc.sbuf_tensor(f"sc{i}", [P, TC], F32)) for i in range(2)]
        gf_rot = [e(nc.sbuf_tensor(f"gf{i}", [P, TC], F32)) for i in range(2)]
        y_rot = [e(nc.sbuf_tensor(f"yy{i}", [P, TC], F32)) for i in range(2)]

        idf = e(nc.sbuf_tensor("idf", [P, P], F32))
        idb = e(nc.sbuf_tensor("idb", [P, P], BF16))
        cols = e(nc.sbuf_tensor("cols", [P, 16 * 12], F32))
        gscc = e(nc.sbuf_tensor("gscc", [P, NI], F32))
        uscc = e(nc.sbuf_tensor("uscc", [P, NI], F32))
        dlruc = e(nc.sbuf_tensor("dlruc", [P, 64], F32))
        dlrvc = e(nc.sbuf_tensor("dlrvc", [4, D], F32))
        maskc = e(nc.sbuf_tensor("maskc", [1, 64], F32))
        onesr = e(nc.sbuf_tensor("onesr", [1, P], F32))
        mbc = e(nc.sbuf_tensor("mbc", [P, 64], F32))
        qm = e(nc.sbuf_tensor("qm", [P, 64], F32))
        jk16 = e(nc.sbuf_tensor("jk16", [P, 16], F32))
        pwa = e(nc.sbuf_tensor("pwa", [P, 16], F32))
        pwb = e(nc.sbuf_tensor("pwb", [P, 16], F32))
        a512 = e(nc.sbuf_tensor("a512", [P, 16], F32))
        pw1 = e(nc.sbuf_tensor("pw1", [P, 16], F32))
        pw2 = e(nc.sbuf_tensor("pw2", [P, 16], F32))
        sin_t = e(nc.sbuf_tensor("sin_t", [P, 16], F32))
        cura = e(nc.sbuf_tensor("cura", [P, 16], F32))
        curb = e(nc.sbuf_tensor("curb", [P, 16], F32))
        rsb = e(nc.sbuf_tensor("rsb", [4, 1], F32))
        carr = e(nc.sbuf_tensor("carr", [P, 16], F32))
        carrT = e(nc.sbuf_tensor("carrT", [16, P], F32))
        ustotT = e(nc.sbuf_tensor("ustotT", [16, P], F32))
        carrh = e(nc.sbuf_tensor("carrh", [P, 16 * 9], F32))
        sf = e(nc.sbuf_tensor("sf", [P, 16 * 8], F32))
        convb = e(nc.sbuf_tensor("convb", [P, 2 * D], BF16))
        vrow = convb.bitcast(F32)[0:1, 0:D]
        usall = e(nc.sbuf_tensor("usall", [P, 128], F32))
        lnst = e(nc.sbuf_tensor("lnst", [P, 16], F32))
        lnbn = e(nc.sbuf_tensor("lnbn", [P, 2 * 24], F32))
        epst = e(nc.sbuf_tensor("epst", [P, 1], F32))
        m128t = e(nc.sbuf_tensor("m128t", [P, 1], F32))
        gcol = hF[0:16, 1024:1024 + D]

        psF = [e(nc.psum_tensor(f"psF{i}", [P, TC], F32)) for i in range(6)]
        psT = [e(nc.psum_tensor(f"psT{i}", [P, 2 * TC], BF16)) for i in range(2)]

        sems = {k: e(nc.semaphore(f"s_{k}")) for k in
                ("pe", "dve", "act", "cc",
                 "cst", "vrw", "xl0", "xl1",
                 "wi0", "wi1", "wi2", "wi3", "wi4", "wi5", "wi6", "wi7",
                 "wg0", "wg1", "wu0", "wu1", "wd0", "wd1",
                 "xnl", "ugl", "cci", "ccg",
                 "xst", "ust", "wo0", "wo1", "xpr",
                 "os0", "os1", "os2", "os3", "os4", "os5", "os6", "os7")}
        block = e(nc.Block())
        n = {k: 0 for k in sems}
        E = _EvDict()

        def tick(key, instr, k=1):
            instr.then_inc(sems[key], k)
            n[key] += k
            return (key, n[key])

        def ev(key):
            return (key, n[key])

        def wte(eng, evt):
            if evt:
                eng.wait_ge(sems[evt[0]], evt[1])

        CA, CB, CC, CD, CGB, CDSC, CCB, CDLR, CDEFF, CSIN, CUM, CUST = range(12)

        def col(idx, j=None):
            if j is None:
                return cols[:, idx * 16:(idx + 1) * 16]
            return cols[:, idx * 16 + j: idx * 16 + j + 1]

        sections = []

        class _SelfSync:

            def __init__(self, eng, key):
                self._eng = eng
                self._key = key
                self._last = -1

            def wait_ge(self, *a, **k):
                return self._eng.wait_ge(*a, **k)

            def __getattr__(self, name):
                real = getattr(self._eng, name)

                def f(*a, **k):
                    if n[self._key] > 0 and n[self._key] != self._last:
                        self._eng.wait_ge(sems[self._key], n[self._key])
                        self._last = n[self._key]
                    return real(*a, **k)
                return f

        _selfsync_key = {"vector": "dve", "scalar": "act"}

        def sect(engname, fn):
            if engname in _selfsync_key:
                key = _selfsync_key[engname]
                orig = fn

                def wrapped(eng, _orig=orig, _key=key):
                    return _orig(_SelfSync(eng, _key))
                sections.append((engname, wrapped))
            else:
                sections.append((engname, fn))

        NVEC = 14

        def pre_sync(s):
            tick("cst", s.dma_start(out=idf[:], in_=identf[:]), 16)
            tick("cst", s.dma_start(out=idb[:], in_=identb[:]), 16)
            tick("cst", s.dma_start(out=maskc[:], in_=masks[:]), 16)
            tick("cst", s.dma_start(
                out=dlruc.ap().rearrange("p (j r) -> p j r", j=ND),
                in_=dlru.rearrange("(j p) r -> p j r", p=P)), 16)
            tick("cst", s.dma_start(out=dlrvc[:], in_=dlrv[:]), 16)
            E["pre_const"] = ev("cst")
            E["xpr"] = tick("xpr", s.dma_start(out=out_d[:, :], in_=x_in[:, :]), 16)
            for i in range(NVEC):
                if i >= 1:
                    wte(s, E[("vtp", i - 1)])
                if i < 6:
                    src = [av, bv, cv, dv, gbv, dscv][i][:]
                elif i < 10:
                    src = gscv[0:1, (i - 6) * D:(i - 5) * D]
                else:
                    src = uscv[0:1, (i - 10) * D:(i - 9) * D]
                E[("vrow", i)] = tick("vrw", s.dma_start(out=vrow[:], in_=src), 16)
        sect("sync", pre_sync)

        def pre_pe(t):
            wte(t, E["pre_const"])
            for i in range(NVEC):
                wte(t, E[("vrow", i)])
                if i >= 2:
                    wte(t, E[("vev", i - 2)])
                for k in range(16):
                    ins = t.transpose(psF[i % 2][:, k:k + 1],
                                      vrow[0:1, k * P:(k + 1) * P],
                                      idf[0:1, 0:1])
                E[("vtp", i)] = tick("pe", ins)
        sect("tensor", pre_pe)

        def pre_dve(v):
            for i in range(NVEC):
                wte(v, E[("vtp", i)])
                if i < 6:
                    dst = col([CA, CB, CC, CD, CGB, CDSC][i])
                elif i < 10:
                    dst = gscc[:, (i - 6) * 16:(i - 5) * 16]
                else:
                    dst = uscc[:, (i - 10) * 16:(i - 9) * 16]
                E[("vev", i)] = tick("dve", v.tensor_copy(dst, psF[i % 2][:, 0:16]))
            tick("dve", v.memset(carr[:], 0.0))
            tick("dve", v.memset(carrh[:, 0:16], 0.0))
            tick("dve", v.memset(onesr[:], 1.0))
            tick("dve", v.memset(epst[:], EPS))
            tick("dve", v.memset(m128t[:], -128.0))
            E["pre_dve"] = tick("dve", v.tensor_tensor(col(CCB), col(CB), col(CC),
                                                       op=ALU.mult))
        sect("vector", pre_dve)

        def pre_act(a):
            wte(a, E[("vev", 0)])
            E["tanh1"] = tick("act", a.activation(lnst[:, 0:16], col(CA), AF.Tanh))
        sect("scalar", pre_act)

        def pre_dve2(v):
            wte(v, E["tanh1"])
            E["aacol"] = tick("dve", v.tensor_copy(col(CA), lnst[:, 0:16]))
        sect("vector", pre_dve2)

        aac = col(CA)

        mats = [
            dict(src=gate_w, scr=wgs, tiles=ND, wid=D, int_=False),
            dict(src=gq, scr=wgfs, tiles=NI, wid=D, int_=True),
            dict(src=uq, scr=wufs, tiles=NI, wid=D, int_=True),
            dict(src=dq, scr=wdfs, tiles=ND, wid=II, int_=True),
        ]
        _sb = 0
        wtiles = []
        for mi, m in enumerate(mats):
            m["segbase"] = _sb
            m["nseg"] = m["wid"] // D
            _sb += m["tiles"] * m["nseg"]
            for j in range(m["tiles"]):
                wtiles.append((mi, j))
        NSEG_ALL = _sb

        wodone = {"s": None}
        wdone = {"s": None}

        def w_stage(gj):
            mi, j = wtiles[gj]
            m = mats[mi]
            if m["wid"] == D:
                slot = gj % 8
                lo = slot * D
            else:
                lo = (gj % 2) * 8192
            stg = hI if m["int_"] else hF
            return stg[:, lo: lo + m["wid"]]

        def w_stage_waits(gj):
            mi, j = wtiles[gj]
            m = mats[mi]
            out = []
            if m["wid"] == D:
                if gj >= 8:
                    out.append(gj - 8)
            else:
                prev = gj - 2
                if prev >= 144:
                    out.append(prev)
                else:
                    base = 0 if gj % 2 == 0 else 4
                    for s2 in range(base, base + 4):
                        cand = [g for g in range(144) if g % 8 == s2]
                        if cand:
                            out.append(max(cand))
            return out

        def w_sync_slice(s, lo, hi):
            if lo == 0:
                wte(s, E["pre_dve"])
            for gj in range(lo, hi):
                mi, j = wtiles[gj]
                m = mats[mi]
                for pg in w_stage_waits(gj):
                    pmi, pj = wtiles[pg]
                    wte(s, E[("wconv", pmi, pj, mats[pmi]["nseg"] - 1)])
                E[("win", mi, j)] = tick(
                    "wi%d" % (gj % 8),
                    s.dma_start(out=w_stage(gj), in_=m["src"][j * P:(j + 1) * P, :]),
                    16)

        def w_pl_slice(p, lo, hi):
            def wout(si):
                if si < 0 or si in wodone["s"]:
                    return
                wodone["s"].add(si)
                for mi2, m2 in enumerate(mats):
                    if m2["segbase"] <= si < m2["segbase"] + m2["tiles"] * m2["nseg"]:
                        rel = si - m2["segbase"]
                        j2, seg2 = rel // m2["nseg"], rel % m2["nseg"]
                        break
                pb = wstr_d[si % 2][:, 0:D]
                for g in range(4):
                    wte(p, E[("wev", si * 4 + g)])
                E[("wout", si)] = tick(
                    "wo%d" % (si % 2),
                    p.dma_start(out=m2["scr"][j2][:, seg2 * D:(seg2 + 1) * D],
                                in_=pb), 16)

            last_si = -1
            for gj in range(lo, hi):
                mi, j = wtiles[gj]
                m = mats[mi]
                for seg in range(m["nseg"]):
                    si = m["segbase"] + j * m["nseg"] + seg
                    wout(si - 2)
                    last_si = si
            wout(last_si - 1)
            wout(last_si)
            if hi == len(wtiles):
                E["W_spl0"] = ev("wo0")
                E["W_spl1"] = ev("wo1")

        def w_conv(eng, key, gj, seg):
            mi, j = wtiles[gj]
            m = mats[mi]
            si = m["segbase"] + j * m["nseg"] + seg
            wte(eng, E[("win", mi, j)])
            if si >= 2:
                wte(eng, E[("wtpl", si - 2)])
            srcap = w_stage(gj)[:, seg * D:(seg + 1) * D]
            dst = convb[:, (si % 2) * D:((si % 2) + 1) * D]
            if m["int_"]:
                if key == "dve":
                    ins = eng.tensor_scalar(dst, srcap, -128.0, None, op0=ALU.add)
                else:
                    ins = eng.activation(dst, srcap, AF.Copy, bias=m128t[:, 0:1])
            else:
                if key == "dve":
                    ins = eng.tensor_copy(dst, srcap)
                else:
                    ins = eng.activation(dst, srcap, AF.Copy)
            E[("wconv", mi, j, seg)] = tick(key, ins)

        def w_dve_slice(v, lo, hi):
            for gj in range(lo, hi):
                mi, j = wtiles[gj]
                m = mats[mi]
                for seg in range(m["nseg"]):
                    si = m["segbase"] + j * m["nseg"] + seg
                    w_conv(v, "dve", gj, seg)
                    if si - 1 >= 0 and (si - 1) not in wdone["s"]:
                        wdone["s"].add(si - 1)
                        pb = wstr_d[(si - 1) % 2][:, 0:D]
                        for g in range(4):
                            gi = (si - 1) * 4 + g
                            if gi % 2 != 0:
                                continue
                            wte(v, E[("wtp", gi)])
                            if si - 1 >= 2:
                                wte(v, E[("wout", si - 3)])
                            E[("wev", gi)] = tick("dve", v.tensor_copy(
                                pb[:, g * TC:(g + 1) * TC], psT[gi % 2][:, 0:TC]))
            si = None
            for gj in range(lo, hi):
                mi, j = wtiles[gj]
                m = mats[mi]
                si = m["segbase"] + j * m["nseg"] + m["nseg"] - 1
            if si is not None and si not in wdone["s"]:
                wdone["s"].add(si)
                pb = wstr_d[si % 2][:, 0:D]
                for g in range(4):
                    gi = si * 4 + g
                    if gi % 2 != 0:
                        continue
                    wte(v, E[("wtp", gi)])
                    if si >= 2:
                        wte(v, E[("wout", si - 2)])
                    E[("wev", gi)] = tick("dve", v.tensor_copy(
                        pb[:, g * TC:(g + 1) * TC], psT[gi % 2][:, 0:TC]))
            if hi == len(wtiles):
                E["W_dve"] = ev("dve")

        def w_pe_slice(t, lo, hi):
            for gj in range(lo, hi):
                mi, j = wtiles[gj]
                m = mats[mi]
                for seg in range(m["nseg"]):
                    si = m["segbase"] + j * m["nseg"] + seg
                    cbuf = si % 2
                    wte(t, E[("wconv", mi, j, seg)])
                    for g in range(4):
                        gi = si * 4 + g
                        if gi >= 2:
                            wte(t, E[("wev", gi - 2)])
                        for q in range(4):
                            kb = g * 4 + q
                            ins = t.transpose(
                                psT[gi % 2][:, q * P:(q + 1) * P],
                                convb[:, cbuf * D + kb * P: cbuf * D + (kb + 1) * P],
                                idb[:])
                        E[("wtp", gi)] = tick("pe", ins)
                    E[("wtpl", si)] = ev("pe")
            if hi == len(wtiles):
                E["W_pe"] = ev("pe")

        def w_act_slice(a, lo, hi):
            for gj in range(lo, hi):
                mi, j = wtiles[gj]
                m = mats[mi]
                for seg in range(m["nseg"]):
                    si = m["segbase"] + j * m["nseg"] + seg
                    pb = wstr_d[si % 2][:, 0:D]
                    for g in range(4):
                        gi = si * 4 + g
                        if gi % 2 != 1:
                            continue
                        wte(a, E[("wtp", gi)])
                        if si >= 2:
                            wte(a, E[("wout", si - 2)])
                        E[("wev", gi)] = tick("act", a.activation(
                            pb[:, g * TC:(g + 1) * TC], psT[gi % 2][:, 0:TC],
                            AF.Copy))
            if hi == len(wtiles):
                E["W_act"] = ev("act")

        def emit_w_slice(lo, hi):
            if lo >= hi:
                return
            sect("sync", lambda s, lo=lo, hi=hi: w_sync_slice(s, lo, hi))
            sect("gpsimd", lambda p, lo=lo, hi=hi: w_pl_slice(p, lo, hi))
            sect("vector", lambda v, lo=lo, hi=hi: w_dve_slice(v, lo, hi))
            sect("tensor", lambda t, lo=lo, hi=hi: w_pe_slice(t, lo, hi))
            sect("scalar", lambda a, lo=lo, hi=hi: w_act_slice(a, lo, hi))

        emit_w_slice(0, ND)

        for c in range(NCH):
            def a_sync(s, c=c):
                for sft in range(NT):
                    g = c * NT + sft
                    if g >= 2:
                        wte(s, E[("a_xn", g - 2)])
                    E[("a_xl", g)] = tick("xl%d" % (sft % 2), s.dma_start(
                        out=x_rot[sft % 2][:],
                        in_=x_in[c * TC + sft * P: c * TC + (sft + 1) * P, :]), 16)
                for j in range(ND):
                    jj = c * ND + j
                    if jj >= 2:
                        wte(s, E[("a_gg", (jj - 2) // ND, (jj - 2) % ND)])
                    if c == 0:
                        wte(s, E[("wout", j)])
                    E[("a_wg", c, j)] = tick(
                        "wg%d" % (j % 2),
                        s.dma_start(out=wstr_g[j % 2][:], in_=wgs[j]), 16)
            sect("sync", a_sync)

            def a_act(a, c=c):
                for half in range(2):
                    g1 = c * NT + 2 * half + 1
                    wte(a, E[("a_ag", g1)])
                    E[("a_sd", c, half)] = tick("act", a.activation(
                        lnst[:, 8 + 2 * half: 10 + 2 * half],
                        lnst[:, 4 * half + 1: 4 * half + 4: 2], AF.Sqrt,
                        bias=epst[:, 0:1]))
                for j in range(ND):
                    jj = c * ND + j
                    wte(a, E[("a_gg", c, j)])
                    if jj >= 2:
                        wte(a, E[("a_ug", (jj - 2) // ND, (jj - 2) % ND)])
                    E[("a_sig", c, j)] = tick("act", a.activation(
                        sg_rot[j % 2][:], psF[j % 4][:], AF.Sigmoid,
                        bias=col(CGB, j)))
            sect("scalar", a_act)

            def a_pe(t, c=c):
                for sft in range(NT):
                    g = c * NT + sft
                    wte(t, E[("a_xn", g)])
                    for grp in range(4):
                        if sft == 0:
                            if c > 0:
                                wte(t, E[("a_sig", c - 1, 12 + grp)])
                        else:
                            wte(t, E[("a_tev", g - 1, grp)])
                        for q in range(4):
                            kb = grp * 4 + q
                            ins = t.transpose(
                                psF[grp][:, q * P:(q + 1) * P],
                                tok_rot[sft % 2][:, kb * P:(kb + 1) * P],
                                idf[:])
                        E[("a_tp", g, grp)] = tick("pe", ins)
                    E[("a_tpl", g)] = ev("pe")
                wte(t, E[("a_tevl", c)])
                for j in range(ND):
                    jj = c * ND + j
                    wte(t, E[("a_wg", c, j)])
                    if j < 4:
                        wte(t, E[("a_tev", c * NT + 3, j % 4)])
                    if jj >= 4:
                        wte(t, E[("a_sig", (jj - 4) // ND, (jj - 4) % ND)])
                    for k in range(ND):
                        ins = t.matmul(psF[j % 4][:],
                                       wstr_g[j % 2][:, k * P:(k + 1) * P],
                                       xnT[:, k * TC:(k + 1) * TC],
                                       start=(k == 0), stop=(k == ND - 1))
                    E[("a_gg", c, j)] = tick("pe", ins)
            sect("tensor", a_pe)

            def a_dve(v, c=c):
                def emit_evacs(v, c, sft):
                    g = c * NT + sft
                    for grp in range(4):
                        wte(v, E[("a_tp", g, grp)])
                        if g == 0 and grp == 0:
                            pass
                        if c > 0 and sft == 0 and grp == 0:
                            wte(v, E[("a_gg", c - 1, ND - 1)])
                            wte(v, E[("a_xnst", c - 1)])
                        dst = xnT.ap().rearrange("p (k t) -> p k t", k=ND)[
                            :, grp * 4:(grp + 1) * 4, sft * P:(sft + 1) * P]
                        src = psF[grp].ap().rearrange(
                            "p (q u) -> p q u", q=4)
                        E[("a_tev", g, grp)] = tick("dve", v.tensor_copy(dst, src))

                for half in range(2):
                    for i2 in range(2):
                        sft = 2 * half + i2
                        g = c * NT + sft
                        wte(v, E[("a_xl", g)])
                        for qq in range(4):
                            tick("dve", v.bn_stats(
                                lnbn[:, sft % 2 * 24 + qq * 6:
                                     sft % 2 * 24 + (qq + 1) * 6],
                                x_rot[sft % 2][:, qq * TC:(qq + 1) * TC]))
                        E[("a_ag", g)] = tick("dve", v.bn_aggr(
                            lnst[:, 4 * half + 2 * i2: 4 * half + 2 * i2 + 2],
                            lnbn[:, sft % 2 * 24: sft % 2 * 24 + 24]))
                    wte(v, E[("a_sd", c, half)])
                    tick("dve", v.reciprocal(lnst[:, 8 + 2 * half: 10 + 2 * half],
                                             lnst[:, 8 + 2 * half: 10 + 2 * half]))
                    for i2 in range(2):
                        sft = 2 * half + i2
                        g = c * NT + sft
                        if g >= 2:
                            wte(v, E[("a_tpl", g - 2)])
                        E[("a_xn", g)] = tick("dve", v.tensor_scalar(
                            tok_rot[sft % 2][:], x_rot[sft % 2][:],
                            lnst[:, 4 * half + 2 * i2: 4 * half + 2 * i2 + 1],
                            lnst[:, 8 + sft: 8 + sft + 1],
                            op0=ALU.subtract, op1=ALU.mult))
                    for t2 in (2 * half, 2 * half + 1):
                        emit_evacs(v, c, t2)
                E[("a_tevl", c)] = ev("dve")
                for j in range(ND):
                    wte(v, E[("a_sig", c, j)])
                    if c > 0 and j == 0:
                        wte(v, E[("a_ugst", c - 1)])
                    E[("a_ug", c, j)] = tick("dve", v.scalar_tensor_tensor(
                        ug[:, j * TC:(j + 1) * TC],
                        xnT[:, j * TC:(j + 1) * TC], 1.0, sg_rot[j % 2][:],
                        op0=ALU.mult, op1=ALU.mult,
                        accum_out=usall[:, j * 8 + c: j * 8 + c + 1]))
                E[("a_ugl", c)] = ev("dve")
                wte(v, ev("dve"))
                for j in range(ND):
                    tick("dve", v.tensor_tensor_scan(
                        sc_rot[j % 2][:], aac[:, j:j + 1].broadcast_to([P, TC]),
                        ug[:, j * TC:(j + 1) * TC], carr[:, j:j + 1],
                        op0=ALU.mult, op1=ALU.add))
                    wte(v, ev("dve"))
                    tick("dve", v.tensor_copy(carr[:, j:j + 1],
                                              sc_rot[j % 2][:, TC - 1:TC]))
                E[("a_carr", c)] = tick("dve", v.tensor_copy(
                    carrh[:, (c + 1) * 16:(c + 2) * 16], carr[:]))
            sect("vector", a_dve)

            def a_spl(p, c=c):
                wte(p, E[("a_tevl", c)])
                E[("a_xnst", c)] = tick("xst",
                                        p.dma_start(out=xnts[c], in_=xnT[:]), 16)
                wte(p, E[("a_ugl", c)])
                E[("a_ugst", c)] = tick("ust",
                                        p.dma_start(out=ugs[c], in_=ug[:]), 16)
            sect("gpsimd", a_spl)

            emit_w_slice(ND + 18 * c, min(ND + 18 * (c + 1), len(wtiles)))

        def b_dve1(v):
            for j in range(ND):
                tick("dve", v.reduce_sum(col(CUST, j),
                                         usall[:, j * 8:(j + 1) * 8], axis=AX))
            E["b_ust"] = ev("dve")
        sect("vector", b_dve1)

        def b_pe1(t):
            wte(t, E[("a_carr", NCH - 1)])
            wte(t, E["b_ust"])
            wte(t, E[("a_sig", NCH - 1, 12)])
            wte(t, E[("a_sig", NCH - 1, 13)])
            ins = t.transpose(psF[0][0:16, 0:P], carrh[:, 8 * 16: 9 * 16], idf[:])
            E["b_ctp"] = tick("pe", ins)
            ins = t.transpose(psF[1][0:16, 0:P], col(CUST), idf[:])
            E["b_utp"] = tick("pe", ins)
        sect("tensor", b_pe1)

        def b_dve2(v):
            wte(v, E["b_ctp"])
            tick("dve", v.tensor_copy(carrT[:], psF[0][0:16, 0:P]))
            wte(v, E["b_utp"])
            tick("dve", v.tensor_copy(ustotT[:], psF[1][0:16, 0:P]))
            E["b_rows"] = ev("dve")
        sect("vector", b_dve2)

        def b_sync1(s):
            wte(s, E["b_rows"])
            tick("cci", s.dma_start(
                out=ccin_ap[0:1, :].rearrange("o (j p) -> j (o p)", p=P),
                in_=carrT[:]), 16)
            tick("cci", s.dma_start(
                out=ccin_ap[1:2, :].rearrange("o (j p) -> j (o p)", p=P),
                in_=ustotT[:]), 16)
            E["b_ccin"] = ev("cci")
        sect("sync", b_sync1)

        def b_cc(p):
            wte(p, E["b_ccin"])
            tick("cc", p.collective_compute(
                "AllGather", ALU.bypass,
                replica_groups=[list(range(8))],
                ins=[ccin_ap.opt()],
                outs=[ccout_ap.opt()]), 1)
        sect("gpsimd", b_cc)

        def b_sync2(s):
            s.wait_ge(sems["cc"], 1)
            E["b_g"] = tick("ccg", s.dma_start(out=gcol, in_=ccout_ap[:, :]), 16)
        sect("sync", b_sync2)

        def b_pe2(t):
            wte(t, E["b_g"])
            wte(t, E[("a_sig", NCH - 1, 14)])
            wte(t, E[("a_sig", NCH - 1, 15)])
            for j in range(ND):
                ins = t.transpose(psF[2][:, j * 16:(j + 1) * 16],
                                  gcol[:, j * P:(j + 1) * P], idf[0:16, 0:16])
                tick("pe", ins)
            E["b_gtp"] = ev("pe")
            for m in range(4):
                ins = t.matmul(psF[3][:, m * 16:(m + 1) * 16], onesr[:],
                               maskc[0:1, m * 16:(m + 1) * 16],
                               start=True, stop=True)
                tick("pe", ins)
            E["b_mbc"] = ev("pe")
        sect("tensor", b_pe2)

        def b_dve3(v):
            wte(v, E["b_gtp"])
            tick("dve", v.tensor_copy(hF[:, 0:256], psF[2][:, 0:256]))
            wte(v, E["b_mbc"])
            tick("dve", v.tensor_copy(mbc[:], psF[3][:, 0:64]))
            E["b_cols"] = ev("dve")
            for m in range(4):
                for j in range(ND):
                    tick("dve", v.scalar_tensor_tensor(
                        jk16[:], hF[:, j * 16:(j + 1) * 16], 1.0,
                        mbc[:, m * 16:(m + 1) * 16],
                        op0=ALU.mult, op1=ALU.mult,
                        accum_out=qm[:, m * 16 + j: m * 16 + j + 1]))
            tick("dve", v.tensor_copy(pwa[:], aac))
            cur, nxt = pwa, pwb
            for step in range(12):
                wte(v, ev("dve"))
                tick("dve", v.tensor_tensor(nxt[:], cur[:], cur[:], op=ALU.mult))
                if step == 8:
                    tick("dve", v.tensor_copy(a512[:], nxt[:]))
                cur, nxt = nxt, cur
            tick("dve", v.tensor_copy(pw1[:], cur[:]))
            wte(v, ev("dve"))
            tick("dve", v.tensor_tensor(pw2[:], pw1[:], pw1[:], op=ALU.mult))
            wte(v, ev("dve"))
            tick("dve", v.tensor_tensor(jk16[:], pw1[:], qm[:, 16:32], op=ALU.mult))
            wte(v, ev("dve"))
            tick("dve", v.tensor_tensor(sin_t[:], qm[:, 0:16], jk16[:], op=ALU.add))
            wte(v, ev("dve"))
            tick("dve", v.tensor_tensor(jk16[:], pw2[:], qm[:, 32:48], op=ALU.mult))
            wte(v, ev("dve"))
            tick("dve", v.tensor_tensor(sin_t[:], sin_t[:], jk16[:], op=ALU.add))
            tick("dve", v.tensor_scalar(col(CUM), qm[:, 48:64], 1.0 / 16384.0,
                                        None, op0=ALU.mult))
            E["b_umean"] = ev("dve")
        sect("vector", b_dve3)

        def b_pe3(t):
            wte(t, E["b_umean"])
            for k in range(ND):
                ins = t.matmul(psF[4][0:4, 0:1], dlruc[:, k * 4:(k + 1) * 4],
                               col(CUM, k), start=(k == 0), stop=(k == ND - 1))
            E["b_r"] = tick("pe", ins)
        sect("tensor", b_pe3)

        def b_dve4(v):
            wte(v, E["b_r"])
            E["b_rs"] = tick("dve", v.tensor_copy(rsb[:], psF[4][0:4, 0:1]))
        sect("vector", b_dve4)

        def b_pe4(t):
            wte(t, E["b_rs"])
            for j in range(ND):
                ins = t.matmul(psF[5][:, j:j + 1],
                               dlrvc[0:4, j * P:(j + 1) * P], rsb[:],
                               start=True, stop=True)
                tick("pe", ins)
            E["b_dlr"] = ev("pe")
        sect("tensor", b_pe4)

        def b_dve5(v):
            wte(v, E["b_dlr"])
            tick("dve", v.tensor_scalar(col(CDLR), psF[5][:, 0:16], 0.25, None,
                                        op0=ALU.mult))
            wte(v, ev("dve"))
            tick("dve", v.tensor_tensor(col(CDEFF), col(CD), col(CDLR),
                                        op=ALU.add))
            tick("dve", v.tensor_copy(cura[:], sin_t[:]))
            cur, nxt = cura, curb
            for c in range(NCH):
                wte(v, ev("dve"))
                tick("dve", v.tensor_tensor(sf[:, c * 16:(c + 1) * 16],
                                            carrh[:, c * 16:(c + 1) * 16],
                                            cur[:], op=ALU.add))
                tick("dve", v.tensor_tensor(nxt[:], cur[:], a512[:], op=ALU.mult))
                cur, nxt = nxt, cur
            E["b_done"] = ev("dve")
        sect("vector", b_dve5)

        for c in range(NCH):
            def c_sync(s, c=c):
                if c == 0:
                    wte(s, E[("a_gg", NCH - 1, ND - 1)])
                    wte(s, E[("a_xnst", NCH - 1)])
                    wte(s, E["W_spl0"])
                    wte(s, E["W_spl1"])
                else:
                    wte(s, E[("c_gu", c - 1, NI - 1)])
                    wte(s, E[("c_dtpl", c - 1)])
                E[("c_xnl", c)] = tick("xnl",
                                       s.dma_start(out=xnT[:], in_=xnts[c]), 16)
                if c == 0:
                    wte(s, E[("a_carr", NCH - 1)])
                    wte(s, E[("a_ugst", NCH - 1)])
                else:
                    wte(s, E[("c_dtpl", c - 1)])
                E[("c_ugl", c)] = tick("ugl",
                                       s.dma_start(out=ug[:], in_=ugs[c]), 16)
                for j in range(NI):
                    jj = c * NI + j
                    if jj >= 2:
                        wte(s, E[("c_gu", (jj - 2) // NI, (jj - 2) % NI)])
                    E[("c_wgl", c, j)] = tick(
                        "wg%d" % (j % 2),
                        s.dma_start(out=wstr_g[j % 2][:], in_=wgfs[j]), 16)
                    E[("c_wul", c, j)] = tick(
                        "wu%d" % (j % 2),
                        s.dma_start(out=wstr_u[j % 2][:], in_=wufs[j]), 16)
                for j2 in range(ND):
                    jj2 = c * ND + j2
                    for hf in range(2):
                        if jj2 >= 1:
                            wte(s, E[("c_dg", (jj2 - 1) // ND, (jj2 - 1) % ND)])
                        hb = (j2 * 2 + hf) % 2
                        E[("c_wdl", c, j2, hf)] = tick(
                            "wd%d" % hb,
                            s.dma_start(out=wstr_d[hb][:],
                                        in_=wdfs[j2][:, hf * 4096:(hf + 1) * 4096]),
                            16)
            sect("sync", c_sync)

            def c_pe(t, c=c):
                wte(t, E[("c_xnl", c)])
                if c == 0:
                    wte(t, E["b_done"])
                for j in range(NI):
                    gb = (j % 2) * 2
                    ub = gb + 1
                    wte(t, E[("c_wgl", c, j)])
                    if j >= 2:
                        wte(t, E[("c_sg", c, j - 2)])
                    elif c > 0:
                        wte(t, E[("c_oev4", c - 1, NT - 1, gb)])
                    for k in range(ND):
                        ins = t.matmul(psF[gb][:],
                                       wstr_g[j % 2][:, k * P:(k + 1) * P],
                                       xnT[:, k * TC:(k + 1) * TC],
                                       start=(k == 0), stop=(k == ND - 1))
                    E[("c_gmm", c, j)] = tick("pe", ins)
                    wte(t, E[("c_wul", c, j)])
                    if j >= 2:
                        wte(t, E[("c_ut", c, j - 2)])
                    elif c > 0:
                        wte(t, E[("c_oev4", c - 1, NT - 1, ub)])
                    for k in range(ND):
                        ins = t.matmul(psF[ub][:],
                                       wstr_u[j % 2][:, k * P:(k + 1) * P],
                                       xnT[:, k * TC:(k + 1) * TC],
                                       start=(k == 0), stop=(k == ND - 1))
                    tick("pe", ins)
                    E[("c_umm", c, j)] = ("pe", n["pe"])
                    E[("c_gu", c, j)] = ("pe", n["pe"])
                def dtp_group(gg):
                    wte(t, E[("c_dl", c, 4 * gg + 3)])
                    for sft in range(NT):
                        if gg == 0:
                            if sft in (0, 2):
                                wte(t, E[("c_sg", c, NI - 2 + (sft // 2))])
                            else:
                                wte(t, E[("c_ut", c, NI - 2 + ((sft - 1) // 2))])
                        else:
                            wte(t, E[("c_oev4", c, gg - 1, sft)])
                        for q in range(4):
                            j2 = gg * 4 + q
                            ins = t.transpose(
                                psF[sft][:, q * P:(q + 1) * P],
                                delta_ap(j2, sft * P, P),
                                idf[:])
                        E[("c_dtp", c, gg, sft)] = tick("pe", ins)

                for j2 in range(ND):
                    jj2 = c * ND + j2
                    db = 4 + j2 % 2
                    if j2 == 0:
                        wte(t, E[("c_h", c, NI - 1)])
                    if jj2 >= 2:
                        wte(t, E[("c_gf", (jj2 - 2) // ND, (jj2 - 2) % ND)])
                    for hf in range(2):
                        wte(t, E[("c_wdl", c, j2, hf)])
                        for kk in range(32):
                            k = hf * 32 + kk
                            ins = t.matmul(
                                psF[db][:],
                                wstr_d[(j2 * 2 + hf) % 2][:, kk * P:(kk + 1) * P],
                                h[:, k * TC:(k + 1) * TC],
                                start=(k == 0), stop=(k == NI - 1))
                    E[("c_dg", c, j2)] = tick("pe", ins)
                    if j2 == 7:
                        dtp_group(0)
                    elif j2 == 11:
                        dtp_group(1)
                    elif j2 == 15:
                        dtp_group(2)
                        dtp_group(3)
                E[("c_dtpl", c)] = ev("pe")
            sect("tensor", c_pe)

            def c_act(a, c=c):
                for j in range(NI):
                    jj = c * NI + j
                    wte(a, E[("c_gmm", c, j)])
                    if jj >= 2:
                        jp = jj - 2
                        wte(a, E[("c_h", jp // NI, jp % NI)])
                    else:
                        wte(a, E[("a_ug", NCH - 1, ND - 2 + j)])
                    E[("c_sg", c, j)] = tick("act", a.activation(
                        sg_rot[j % 2][:], psF[(j % 2) * 2][:], AF.Silu,
                        scale=gscc[:, j:j + 1]))
                for j2 in range(ND):
                    jj2 = c * ND + j2
                    wte(a, E[("c_dg", c, j2)])
                    if jj2 >= 2:
                        wte(a, E[("c_dl", (jj2 - 2) // ND, (jj2 - 2) % ND)])
                    E[("c_gf", c, j2)] = tick("act", a.activation(
                        gf_rot[j2 % 2][:], psF[4 + j2 % 2][:], AF.Silu,
                        scale=col(CDSC, j2)))
            sect("scalar", c_act)

            def c_dve(v, c=c):
                if c == 0:
                    wte(v, E["b_done"])
                for j in range(NI):
                    wte(v, E[("c_umm", c, j)])
                    E[("c_ut", c, j)] = tick("dve", v.tensor_scalar(
                        ut_rot[j % 2][:], psF[(j % 2) * 2 + 1][:],
                        uscc[:, j:j + 1], None, op0=ALU.mult))
                    wte(v, E[("c_sg", c, j)])
                    if j == 0 and c > 0:
                        wte(v, E[("c_dg", c - 1, ND - 1)])
                    E[("c_h", c, j)] = tick("dve", v.tensor_tensor(
                        h[:, j * TC:(j + 1) * TC], sg_rot[j % 2][:],
                        ut_rot[j % 2][:], op=ALU.mult))
                wte(v, E[("c_ugl", c)])

                def dve_oev(gg):
                    for sft in range(NT):
                        wte(v, E[("c_dtp", c, gg, sft)])
                        if E.get(("c_ost", c, gg - 2, sft)):
                            wte(v, E[("c_ost", c, gg - 2, sft)])
                        elif c > 0 and gg < 2:
                            wte(v, E[("c_ost", c - 1, gg + 2, sft)])
                        E[("c_oev4", c, gg, sft)] = tick("dve", v.tensor_copy(
                            tok_rot[gg % 2][:, sft * TC:(sft + 1) * TC],
                            psF[sft][:]))

                for gg in range(4):
                    for q in range(4):
                        j2 = gg * 4 + q
                        wte(v, ev("dve"))
                        tick("dve", v.tensor_tensor_scan(
                            sc_rot[j2 % 2][:],
                            aac[:, j2:j2 + 1].broadcast_to([P, TC]),
                            ug[:, j2 * TC:(j2 + 1) * TC],
                            sf[:, c * 16 + j2: c * 16 + j2 + 1],
                            op0=ALU.mult, op1=ALU.add))
                        wte(v, ev("dve"))
                        tick("dve", v.tensor_scalar(
                            y_rot[j2 % 2][:], sc_rot[j2 % 2][:], col(CCB, j2),
                            None, op0=ALU.mult))
                        E[("c_y", c, j2)] = tick("dve", v.scalar_tensor_tensor(
                            sc_rot[j2 % 2][:], ug[:, j2 * TC:(j2 + 1) * TC],
                            col(CDEFF, j2), y_rot[j2 % 2][:],
                            op0=ALU.mult, op1=ALU.add))
                        wte(v, E[("c_gf", c, j2)])
                        E[("c_dl", c, j2)] = tick("dve", v.tensor_tensor(
                            delta_ap(j2, 0, TC), sc_rot[j2 % 2][:],
                            gf_rot[j2 % 2][:], op=ALU.mult))
                    if gg >= 1:
                        dve_oev(gg - 1)
                dve_oev(3)
            sect("vector", c_dve)

            def c_spl(p, c=c):
                if c == 0:
                    wte(p, E["xpr"])
                for gg in range(4):
                    for sft in range(NT):
                        wte(p, E[("c_oev4", c, gg, sft)])
                        E[("c_ost", c, gg, sft)] = tick(
                            "os%d" % ((gg % 2) * 4 + sft), p.dma_start(
                                out=out_d[c * TC + sft * P:
                                          c * TC + (sft + 1) * P,
                                          gg * TC:(gg + 1) * TC],
                                in_=tok_rot[gg % 2][:, sft * TC:(sft + 1) * TC],
                                accum_op=ALU.add), 16)
            sect("gpsimd", c_spl)

        def tail(p):
            for k in ("os0", "os1", "os2", "os3", "os4", "os5", "os6", "os7",
                      "xst", "ust", "wo0", "wo1", "xpr"):
                p.wait_ge(sems[k], n[k])
        sect("gpsimd", tail)

        stub = _StubEngine()
        wdone["s"] = set()
        wodone["s"] = set()
        for engname, fn in sections:
            fn(stub)
        pass1 = dict(n)
        for k in n:
            n[k] = 0
        wdone["s"] = set()
        wodone["s"] = set()
        E.pass2 = True
        for engname, fn in sections:
            getattr(block, engname)(fn)
        assert n == pass1, f"two-pass mismatch: {n} vs {pass1}"

    return nc


_NC_CACHE = None


def kernel(**inputs):
    global _NC_CACHE
    x = np.ascontiguousarray(inputs["L0_attn_u"])
    B, T, Dm = x.shape
    assert (B, T, Dm) == (2, 16384, 2048)

    def row(v):
        return np.ascontiguousarray(np.asarray(v, dtype=np.float32).reshape(1, -1))

    shared = dict(
        av=row(inputs["L0_attn_a"]),
        bv=row(inputs["L0_attn_b"]),
        cv=row(inputs["L0_attn_c"]),
        dv=row(inputs["L0_attn_d"]),
        gbv=row(inputs["L0_attn_gate_b"]),
        dscv=row(inputs["L0_ffn_down_scale"]),
        gscv=row(inputs["L0_ffn_gate_scale"]),
        uscv=row(inputs["L0_ffn_up_scale"]),
        gate_w=np.ascontiguousarray(inputs["L0_attn_gate_w"], dtype=np.float32),
        gq=np.ascontiguousarray(inputs["L0_ffn_gate_w"], dtype=np.int32),
        uq=np.ascontiguousarray(inputs["L0_ffn_up_w"], dtype=np.int32),
        dq=np.ascontiguousarray(inputs["L0_ffn_down_w"], dtype=np.int32),
        dlru=np.ascontiguousarray(inputs["L0_attn_d_lr_u"], dtype=np.float32),
        dlrv=np.ascontiguousarray(inputs["L0_attn_d_lr_v"], dtype=np.float32),
        identf=np.eye(P, dtype=np.float32),
        identb=np.eye(P, dtype=np.float32).astype(ml_dtypes.bfloat16),
    )

    in_maps = []
    for core in range(8):
        b, q = core // 4, core % 4
        msk = np.zeros((1, 64), dtype=np.float32)
        for j in range(4):
            m = q - 1 - j
            if 0 <= m <= 2:
                msk[0, m * 16 + 2 * (4 * b + j)] = 1.0
            msk[0, 48 + 2 * (4 * b + j) + 1] = 1.0
        im = dict(shared)
        im["x"] = np.ascontiguousarray(x[b, q * TCORE:(q + 1) * TCORE, :])
        im["masks"] = msk
        in_maps.append(im)

    if _NC_CACHE is None:
        _NC_CACHE = build_nc()
    res = run_bass_kernel_spmd(_NC_CACHE, in_maps, core_ids=list(range(8)))

    out = np.empty((B, T, Dm), dtype=np.float32)
    for core in range(8):
        b, q = core // 4, core % 4
        out[b, q * TCORE:(q + 1) * TCORE, :] = res.results[core]["out"]
    return out
